# version 01
# baseline (speedup 1.0000x reference)
import numpy as np
import ml_dtypes
import concourse.bass as bass
import concourse.mybir as mybir
from concourse.bass_utils import run_bass_kernel_spmd
from contextlib import ExitStack

F32 = mybir.dt.float32
BF16 = mybir.dt.bfloat16
AF = mybir.ActivationFunctionType
ALU = mybir.AluOpType
AX = mybir.AxisListType.X

NCORES = 8
M = 2048
D = 1024
H = 16
DH = 64
CH = 128
NCH = M // CH
SCALE = float(DH ** (-0.25))
E = 8
GC = 1.0 / float(np.e)
GL = 0.5 - float(np.log(4.0 / 3.0))
GU = 0.5
EPS = 1e-5

ENGINES = ("sync", "gpsimd", "vector", "scalar", "tensor")


class Buf:
    __slots__ = ("writer", "readers", "name")

    def __init__(self, name=""):
        self.writer = None
        self.readers = []
        self.name = name


class PB:

    def __init__(self):
        self.ops = {e: [] for e in ENGINES}
        self.pending = {e: [] for e in ENGINES}

    def _add(self, eng, fn, deps, is_dma, fence, is_async):
        idx = len(self.ops[eng])
        deps = list(deps) + self.pending[eng]
        self.pending[eng] = []
        self.ops[eng].append({
            "fn": fn, "deps": deps, "dma": is_dma, "fence": fence,
            "async": is_async or is_dma, "mark": False,
        })
        return (eng, idx)

    SERIAL = set()

    def op(self, eng, fn, reads=(), writes=(), deps=(), fence=False, dma=False,
           is_async=False):
        if "all" in PB.SERIAL or getattr(self, "region", "") in PB.SERIAL:
            self.barrier()
            fence = True
        if eng in ("vector", "scalar", "gpsimd"):
            fence = True
        if "fence" in PB.SERIAL or ("f" + eng[0]) in PB.SERIAL:
            fence = True
        if "bar" in PB.SERIAL:
            self.barrier()
        dd = list(deps)
        for b in reads:
            if b.writer is not None:
                dd.append(b.writer)
        for b in writes:
            if b.writer is not None:
                dd.append(b.writer)
            dd.extend(b.readers)
        tok = self._add(eng, fn, dd, dma, fence, is_async)
        for b in reads:
            b.readers.append(tok)
        for b in writes:
            b.readers = []
            b.writer = tok
        return tok

    def barrier(self):
        last = []
        for e in ENGINES:
            if self.ops[e]:
                last.append((e, len(self.ops[e]) - 1))
        for e in ENGINES:
            self.pending[e].extend(last)

    def finalize_and_emit(self, nc, ctx):
        for e in ENGINES:
            for i, o in enumerate(self.ops[e]):
                seen = {}
                for (de, di) in o["deps"]:
                    if de == e and di >= i:
                        continue
                    if (de == e and not o["fence"]
                            and not self.ops[de][di]["async"]):
                        continue
                    seen[de] = max(seen.get(de, -1), di)
                o["waits"] = seen
                for de, di in seen.items():
                    self.ops[de][di]["mark"] = True
        for e in ENGINES:
            for o in self.ops[e]:
                if o["dma"]:
                    o["mark"] = True
        semval = {e: [] for e in ENGINES}
        for e in ENGINES:
            v = 0
            for o in self.ops[e]:
                if o["mark"]:
                    v += 16 if o["dma"] else 1
                semval[e].append(v)
        sems = {e: ctx.enter_context(nc.semaphore("sem_" + e)) for e in ENGINES}
        block = ctx.enter_context(nc.Block())

        def make_section(ename):
            def section(eng):
                water = {e: 0 for e in ENGINES}
                for i, o in enumerate(self.ops[ename]):
                    for de, di in sorted(o["waits"].items()):
                        v = semval[de][di]
                        if v > water[de]:
                            eng.wait_ge(sems[de], v)
                            water[de] = v
                    inst = o["fn"](eng)
                    if o["mark"]:
                        inst.then_inc(sems[ename], 16 if o["dma"] else 1)
            return section

        getattr(block, "sync")(make_section("sync"))
        getattr(block, "gpsimd")(make_section("gpsimd"))
        getattr(block, "vector")(make_section("vector"))
        getattr(block, "scalar")(make_section("scalar"))
        getattr(block, "tensor")(make_section("tensor"))


def build_program(debug=False):
    import os
    skips = set((os.environ.get("KSKIP") or "").split(","))
    PB.SERIAL = set((os.environ.get("KSERIAL") or "").split(","))
    scanlim = os.environ.get("SCANLIM", "full")
    LVL = {"s": 0, "den": 1, "o": 2, "div": 3, "full": 9}[scanlim]
    nc = bass.Bass("TRN2", target_bir_lowering=False, debug=False)

    x_in = nc.dram_tensor("x", [M, D], F32, kind="ExternalInput").ap()
    wqkv = nc.dram_tensor("wqkv", [D, 3 * D], F32, kind="ExternalInput").ap()
    wout = nc.dram_tensor("wout", [D, D], F32, kind="ExternalInput").ap()
    wg_h = nc.dram_tensor("wg", [E, D, D], F32, kind="ExternalInput").ap()
    wu_h = nc.dram_tensor("wu", [E, D, D], F32, kind="ExternalInput").ap()
    wd_h = nc.dram_tensor("wd", [E, D, D], F32, kind="ExternalInput").ap()
    wr_h = nc.dram_tensor("wr", [D, E], F32, kind="ExternalInput").ap()
    temp_h = nc.dram_tensor("temp1", [128, 1], F32, kind="ExternalInput").ap()
    cos_h = nc.dram_tensor("cosr", [M, 32], F32, kind="ExternalInput").ap()
    sin_h = nc.dram_tensor("sinr", [M, 32], F32, kind="ExternalInput").ap()
    identb_h = nc.dram_tensor("identb", [128, 128], BF16, kind="ExternalInput").ap()
    identf_h = nc.dram_tensor("identf", [128, 128], F32, kind="ExternalInput").ap()
    masku4_h = nc.dram_tensor("masku4", [128, 512], F32, kind="ExternalInput").ap()
    onesb_h = nc.dram_tensor("onesb", [128, 1], BF16, kind="ExternalInput").ap()
    esel4_h = nc.dram_tensor("esel4", [4, 256], F32, kind="ExternalInput").ap()
    e8sel_h = nc.dram_tensor("e8sel", [8, 1024], F32, kind="ExternalInput").ap()
    prefm_h = nc.dram_tensor("prefmask", [64, 8], F32, kind="ExternalInput").ap()
    consts_h = nc.dram_tensor("consts", [128, 8], F32, kind="ExternalInput").ap()
    out_h = nc.dram_tensor("out", [M, D], F32, kind="ExternalOutput").ap()

    dbg = {}
    if debug:
        for nm, shape in [("d_xnfm", [128, 1024]), ("d_ktm", [128, 1024]),
                          ("d_vtm", [128, 1024]), ("d_kv", [64, 1024]),
                          ("d_ks", [64, 16]), ("d_attn", [64, 2048]),
                          ("d_x1", [128, 1024]), ("d_wfin", [128, 8]),
                          ("d_den", [64, 512]), ("d_qfm", [64, 2048]),
                          ("d_denr", [128, 16]), ("d_rt", [4, 128]),
                          ("d_smask", [128, 512]), ("d_denA", [128, 4]),
                          ("d_rawO", [64, 512]), ("d_rawden", [128, 68]),
                          ("d_denA2", [128, 4]), ("d_ksbf", [128, 16]),
                          ("d_qfmp", [128, 512]), ("d_kvbfp", [128, 128])]:
            dbg[nm] = nc.dram_tensor(nm, shape, F32, kind="ExternalOutput").ap()

    kv_loc = nc.dram_tensor("kv_loc", [64, 1040], F32, kind="Internal")
    kv_all = nc.dram_tensor("kv_all", [NCORES * 64, 1040], F32, kind="Internal")

    pb = PB()
    ctx = ExitStack()
    out_toks = []

    sb = lambda name, shape, dt: ctx.enter_context(nc.sbuf_tensor(name, shape, dt))
    ps = lambda name, shape, dt: ctx.enter_context(nc.psum_tensor(name, shape, dt))

    x_bf = sb("x_bf", [128, NCH * D], BF16)
    identb = sb("s_identb", [128, 128], BF16)
    identf = sb("s_identf", [128, 128], F32)
    masku4 = sb("s_masku4", [128, 512], F32)
    cos_sb = sb("cos_sb", [128, NCH * 32], F32)
    sin_sb = sb("sin_sb", [128, NCH * 32], F32)
    onesb = sb("s_onesb", [128, 1], BF16)
    esel4 = sb("s_esel4", [4, 256], F32)
    e8sel = sb("s_e8sel", [8, 1024], F32)
    prefm = sb("prefm", [64, 8], F32)
    temp_sb = sb("temp_sb", [128, 1], F32)
    csts = sb("csts", [128, 8], F32)
    C_INVD = lambda: csts[:, 0:1]
    C_EPS = lambda: csts[:, 1:2]
    C_SCALE = lambda: csts[:, 2:3]
    C_N10 = lambda: csts[:, 3:4]
    C_N103 = lambda: csts[:, 4:5]
    KV32 = sb("KV32", [64, 1024], F32)
    ks32 = sb("ks32", [64, 16], F32)
    KVbf = sb("KVbf", [128, 1024], BF16)
    ksbf = sb("ksbf", [128, 16], BF16)
    ln_cen = sb("ln_cen", [128, D], F32)
    ln_sq = sb("ln_sq", [128, D], F32)
    ln_m = sb("ln_m", [128, 1], F32)
    ln_ss = sb("ln_ss", [128, 1], F32)
    ln_rs = sb("ln_rs", [128, 1], F32)
    xn_bf = sb("xn_bf", [128, D], BF16)
    xn_fm = sb("xn_fm", [128, D], BF16)

    b_x = [Buf("x%d" % c) for c in range(NCH)]
    b_const = Buf("const")
    b_kv32 = Buf("kv32")
    b_ks32 = Buf("ks32")
    b_kvbf = Buf("kvbf")
    b_ksbf = Buf("ksbf")
    b_ln = Buf("ln")
    b_xnbf = Buf("xnbf")
    b_xnfm = Buf("xnfm")

    OP = pb.op

    def dma(eng_name, out_ap, in_ap, reads=(), writes=(), deps=(), fence=False):
        return OP(eng_name, lambda e, o=out_ap, i=in_ap: e.dma_start(out=o, in_=i),
                  reads=reads, writes=writes, deps=deps, fence=fence, dma=True)

    dma("sync", identb[:, :], identb_h, writes=[b_const])
    dma("sync", identf[:, :], identf_h, writes=[b_const])
    dma("sync", masku4[:, :], masku4_h, writes=[b_const])
    dma("sync", cos_sb[:, :].rearrange("p (c j) -> p c j", j=32),
        cos_h.rearrange("(c p) j -> p c j", p=128), writes=[b_const])
    dma("sync", sin_sb[:, :].rearrange("p (c j) -> p c j", j=32),
        sin_h.rearrange("(c p) j -> p c j", p=128), writes=[b_const])
    dma("sync", onesb[:, :], onesb_h, writes=[b_const])
    dma("sync", esel4[:, :], esel4_h, writes=[b_const])
    dma("sync", e8sel[:, :], e8sel_h, writes=[b_const])
    dma("sync", prefm[:, :], prefm_h, writes=[b_const])
    dma("sync", temp_sb[:, :], temp_h, writes=[b_const])
    dma("sync", csts[:, :], consts_h, writes=[b_const])
    OP("vector", lambda e: e.memset(KVbf[:, :], 0.0), writes=[b_kvbf])
    OP("vector", lambda e: e.memset(ksbf[:, :], 0.0), writes=[b_ksbf])
    if "xdma" not in skips:
        tok_x = dma("gpsimd", x_bf[:, :].rearrange("p (c d) -> p c d", d=D),
                    x_in.rearrange("(c p) d -> p c d", p=128), writes=b_x)

    def emit_ln(c, ps_tp, b_pstp):
        xt = x_bf[:, D * c:D * (c + 1)]
        OP("vector", lambda e: e.reduce_sum(ln_m[:, :], xt, axis=AX),
           reads=[b_x[c]], writes=[b_ln])
        OP("vector", lambda e: e.tensor_scalar(out=ln_m[:, :], in0=ln_m[:, :],
                                               scalar1=-1.0 / D, scalar2=None, op0=ALU.mult),
           reads=[b_ln], writes=[b_ln])
        OP("scalar", lambda e: e.activation(out=ln_cen[:, :], in_=xt,
                                            func=AF.Identity, bias=ln_m[:, :]),
           reads=[b_x[c], b_ln], writes=[b_ln], fence=True)
        OP("scalar", lambda e: e.activation(out=ln_sq[:, :], in_=ln_cen[:, :],
                                            func=AF.Square, accum_out=ln_ss[:, :]),
           reads=[b_ln], writes=[b_ln], fence=True)
        OP("scalar", lambda e: e.activation(out=ln_rs[:, :], in_=ln_ss[:, :],
                                            func=AF.Sqrt, scale=C_INVD(), bias=C_EPS()),
           reads=[b_ln, b_const], writes=[b_ln], fence=True)
        OP("vector", lambda e: e.reciprocal(ln_rs[:, :], ln_rs[:, :]),
           reads=[b_ln], writes=[b_ln])
        OP("vector", lambda e: e.tensor_scalar(out=xn_bf[:, :], in0=ln_cen[:, :],
                                               scalar1=ln_rs[:, :], scalar2=None, op0=ALU.mult),
           reads=[b_ln], writes=[b_xnbf], fence=True)
        for dc in range(8):
            OP("tensor", lambda e, dc=dc: e.transpose(
                ps_tp[:, 128 * dc:128 * (dc + 1)], xn_bf[:, 128 * dc:128 * (dc + 1)],
                identb[:, :]),
               reads=[b_xnbf, b_const], writes=[b_pstp])
        OP("vector", lambda e: e.tensor_copy(out=xn_fm[:, :], in_=ps_tp[:, 0:D]),
           reads=[b_pstp], writes=[b_xnfm])

    def emit_ropefeat(c, z32, b_z, za, zb, b_sc, zbf, b_zbf):
        def v3(t, off):
            return t[:, :].rearrange("p (h d) -> p h d", d=DH)[:, :, off:off + 32]

        cs = cos_sb[:, :].rearrange("p (cc j) -> p cc j", j=32)[:, c:c + 1, :]
        sn = sin_sb[:, :].rearrange("p (cc j) -> p cc j", j=32)[:, c:c + 1, :]
        za3 = za[:, 0:512].rearrange("p (h j) -> p h j", j=32)
        zb3 = zb[:, 0:512].rearrange("p (h j) -> p h j", j=32)

        def tt3(e, out_ap, a_ap, b_ap, op):
            aa, bb = bass.broadcast_tensor_aps(a_ap, b_ap)
            return e.tensor_tensor(out=out_ap, in0=aa, in1=bb, op=op)

        q1 = v3(z32, 0)
        q2 = v3(z32, 32)
        OP("vector", lambda e: tt3(e, za3, q1, cs, ALU.mult),
           reads=[b_z, b_const], writes=[b_sc])
        OP("vector", lambda e: tt3(e, zb3, q2, sn, ALU.mult),
           reads=[b_z, b_const], writes=[b_sc])
        OP("vector", lambda e: e.tensor_tensor(out=za[:, 0:512], in0=za[:, 0:512],
                                               in1=zb[:, 0:512], op=ALU.subtract),
           reads=[b_sc], writes=[b_sc])
        OP("vector", lambda e: tt3(e, zb3, q1, sn, ALU.mult),
           reads=[b_z, b_const], writes=[b_sc])
        OP("vector", lambda e: tt3(e, v3(z32, 32), v3(z32, 32), cs, ALU.mult),
           reads=[b_z, b_const], writes=[b_z])
        OP("vector", lambda e: e.tensor_tensor(
            out=v3(z32, 32), in0=v3(z32, 32), in1=zb3, op=ALU.add),
           reads=[b_z, b_sc], writes=[b_z])
        OP("vector", lambda e: e.tensor_copy(out=v3(z32, 0), in_=za3),
           reads=[b_sc], writes=[b_z])
        OP("vector", lambda e: e.tensor_scalar(out=za[:, 0:D], in0=z32[:, :],
                                               scalar1=0.0, scalar2=None, op0=ALU.min),
           reads=[b_z], writes=[b_sc])
        OP("scalar", lambda e: e.activation(out=zb[:, 0:D], in_=za[:, 0:D],
                                            func=AF.Exp, scale=C_SCALE()),
           reads=[b_sc, b_const], writes=[b_sc])
        OP("scalar", lambda e: e.activation(out=za[:, 0:D], in_=z32[:, :],
                                            func=AF.Relu, scale=C_SCALE()),
           reads=[b_z, b_const], writes=[b_sc])
        OP("vector", lambda e: e.tensor_tensor(out=zbf[:, :], in0=za[:, 0:D],
                                               in1=zb[:, 0:D], op=ALU.add),
           reads=[b_sc], writes=[b_zbf])

    p12 = ExitStack()
    k_tm = p12.enter_context(nc.sbuf_tensor("k_tm", [128, NCH * D], BF16))
    v_tm = p12.enter_context(nc.sbuf_tensor("v_tm", [128, NCH * D], BF16))
    pb.region = "p1"
    with ExitStack() as p1:
        sb1 = lambda name, shape, dt: p1.enter_context(nc.sbuf_tensor(name, shape, dt))
        ps1 = lambda name, shape, dt: p1.enter_context(nc.psum_tensor(name, shape, dt))
        wkv_bf = sb1("wkv_bf", [128, 8 * 2048], BF16)
        kc32 = sb1("kc32", [128, D], F32)
        rp_a = sb1("rp_a", [128, D], F32)
        rp_b = sb1("rp_b", [128, D], F32)
        ps_tp1 = ps1("ps_tp1", [128, 1024], BF16)
        ps_kvp = ps1("ps_kvp", [128, 1024], F32)
        ps_kv1 = ps1("ps_kv1", [64, 1024], F32)
        ps_ks1 = ps1("ps_ks1", [64, 16], F32)

        b_wkv = Buf("wkv")
        b_ktm = [Buf("ktm%d" % c) for c in range(NCH)]
        b_vtm = [Buf("vtm%d" % c) for c in range(NCH)]
        b_kc = Buf("kc")
        b_rp = Buf("rp")
        b_tp1 = Buf("tp1")
        b_kvp = Buf("kvp")
        b_pkv = Buf("pkv")
        b_pks = Buf("pks")

        if "wdma" not in skips:
            dma("gpsimd", wkv_bf[:, :].rearrange("p (dc n) -> p dc n", n=2048),
                wqkv.rearrange("(dc p) n -> p dc n", p=128)[:, :, D:3 * D],
                writes=[b_wkv])

        for c in range(0 if "p1" in skips else NCH):
            emit_ln(c, ps_tp1, b_tp1)
            for which, dst, b_dst, want32 in (("k", None, None, True), ("v", None, None, False)):
                base = 0 if which == "k" else 1024
                for j in range(2):
                    for dc in range(8):
                        OP("tensor", lambda e, j=j, dc=dc, base=base: e.matmul(
                            ps_kvp[:, 512 * j:512 * (j + 1)],
                            xn_fm[:, 128 * dc:128 * (dc + 1)],
                            wkv_bf[:, 2048 * dc + base + 512 * j: 2048 * dc + base + 512 * (j + 1)],
                            start=(dc == 0), stop=(dc == 7)),
                           reads=[b_xnfm, b_wkv], writes=[b_kvp])
                if which == "k":
                    OP("scalar", lambda e: e.copy(out=kc32[:, :], in_=ps_kvp[:, 0:D]),
                       reads=[b_kvp], writes=[b_kc])
                else:
                    OP("vector", lambda e, c=c: e.tensor_copy(
                        out=v_tm[:, D * c:D * (c + 1)], in_=ps_kvp[:, 0:D]),
                       reads=[b_kvp], writes=[b_vtm[c]])
            emit_ropefeat(c, kc32, b_kc, rp_a, rp_b, b_rp,
                          k_tm[:, D * c:D * (c + 1)], b_ktm[c])
            for h in range(H):
                OP("tensor", lambda e, c=c, h=h: e.matmul(
                    ps_kv1[:, 64 * h:64 * (h + 1)],
                    k_tm[:, D * c + 64 * h: D * c + 64 * (h + 1)],
                    v_tm[:, D * c + 64 * h: D * c + 64 * (h + 1)],
                    start=(c == 0 and (h == 0 or h == 8)), stop=(c == NCH - 1)),
                   reads=[b_ktm[c], b_vtm[c]], writes=[b_pkv])
                OP("tensor", lambda e, c=c, h=h: e.matmul(
                    ps_ks1[:, h:h + 1],
                    k_tm[:, D * c + 64 * h: D * c + 64 * (h + 1)],
                    onesb[:, :],
                    start=(c == 0 and h == 0), stop=(c == NCH - 1)),
                   reads=[b_ktm[c], b_const], writes=[b_pks])

        OP("vector", lambda e: e.tensor_copy(out=KV32[:, :], in_=ps_kv1[:, :]),
           reads=[b_pkv], writes=[b_kv32])
        OP("vector", lambda e: e.tensor_copy(out=ks32[:, :], in_=ps_ks1[:, :]),
           reads=[b_pks], writes=[b_ks32])
        if debug:
            dma("gpsimd", dbg["d_ktm"], k_tm[:, 0:D], reads=[b_ktm[0]])
            dma("gpsimd", dbg["d_vtm"], v_tm[:, 0:D], reads=[b_vtm[0]])
        pb.barrier()

    pb.region = "ex"
    with ExitStack() as pex:
      if "ex" not in skips:
          slabs = pex.enter_context(nc.sbuf_tensor("slabs", [64, 8 * 1040], F32))
          kvoff = pex.enter_context(nc.sbuf_tensor("kvoff", [64, 1040], F32))
          tok_kv_out = dma("sync", kv_loc.ap()[:, 0:1024], KV32[:, :], reads=[b_kv32])
          tok_ks_out = dma("sync", kv_loc.ap()[:, 1024:1040], ks32[:, :], reads=[b_ks32])
          if "excc" not in skips:
              tok_cc = OP("gpsimd", lambda e: e.collective_compute(
                  "AllGather", ALU.bypass, replica_groups=[list(range(NCORES))],
                  ins=[kv_loc.ap()], outs=[kv_all.ap()]),
                  deps=[tok_kv_out, tok_ks_out], is_async=True)
          else:
              tok_cc = OP("gpsimd", lambda e: e.engine_nop(),
                          deps=[tok_kv_out, tok_ks_out])
          b_slab = Buf("slabs")
          b_kvoff = Buf("kvoff")
          if "exsl" not in skips:
              dma("gpsimd", slabs[:, :].rearrange("p (r n) -> p r n", n=1040),
                  kv_all.ap().rearrange("(r p) n -> p r n", p=64),
                  deps=[tok_cc], writes=[b_slab])
          else:
              dma("gpsimd", slabs[:, 0:1040], kv_all.ap()[0:64, :],
                  deps=[tok_cc], writes=[b_slab])
          for j in range(NCORES):
              if j == 0:
                  OP("vector", lambda e: e.tensor_scalar(
                      out=kvoff[:, :], in0=slabs[:, 0:1040],
                      scalar1=prefm[:, 0:1], scalar2=None, op0=ALU.mult),
                     reads=[b_slab, b_const], writes=[b_kvoff])
              else:
                  OP("vector", lambda e, j=j: e.scalar_tensor_tensor(
                      out=kvoff[:, :], in0=slabs[:, 1040 * j:1040 * (j + 1)],
                      scalar=prefm[:, j:j + 1], in1=kvoff[:, :],
                      op0=ALU.mult, op1=ALU.add),
                     reads=[b_slab, b_const, b_kvoff], writes=[b_kvoff])
          OP("vector", lambda e: e.tensor_copy(out=KV32[:, :], in_=kvoff[:, 0:1024]),
             reads=[b_kvoff], writes=[b_kv32])
          OP("vector", lambda e: e.tensor_copy(out=ks32[:, :], in_=kvoff[:, 1024:1040]),
             reads=[b_kvoff], writes=[b_ks32])
          OP("vector", lambda e: e.tensor_copy(out=KVbf[0:64, :], in_=KV32[:, :]),
             reads=[b_kv32], writes=[b_kvbf])
          OP("vector", lambda e: e.tensor_copy(out=ksbf[0:64, :], in_=ks32[:, :]),
             reads=[b_ks32], writes=[b_ksbf])
          if debug:
              dma("sync", dbg["d_kv"], KV32[:, :], reads=[b_kv32])
              dma("sync", dbg["d_ks"], ks32[:, :], reads=[b_ks32])
          pb.barrier()

    pb.region = "p2"
    NCH2 = 0 if "p2" in skips else NCH
    with ExitStack() as p2:
        sb2 = lambda name, shape, dt: p2.enter_context(nc.sbuf_tensor(name, shape, dt))
        ps2 = lambda name, shape, dt: p2.enter_context(nc.psum_tensor(name, shape, dt))
        wq_bf = sb2("wq_bf", [128, 8 * 1024], BF16)
        wo_bf = sb2("wo_bf", [128, 16 * 1024], BF16)
        qc32 = sb2("qc32", [128, D], F32)
        rq_a = sb2("rq_a", [128, D], F32)
        rq_b = sb2("rq_b", [128, D], F32)
        q_bf = sb2("q_bf", [128, D], BF16)
        qfm = sb2("qfm", [128, H * 128], BF16)
        kfm = sb2("kfm", [64, H * 128], BF16)
        smask = sb2("smask", [128, 512], BF16)
        denr = sb2("denr", [128, 16], F32)
        denA = sb2("denA", [128, 4], F32)
        rT = sb2("rT", [4, 128], F32)
        divs = sb2("divs", [64, 512], F32)
        attn_c = sb2("attn_c", [128, H * 128], BF16)
        o1_bf = sb2("o1_bf", [128, D], BF16)

        ps_tp2 = ps2("ps_tp2", [128, 1024], BF16)
        ps_big = ps2("ps_big", [128, 1024], F32)
        ps_S = ps2("ps_S", [128, 512], F32)
        ps_dr = ps2("ps_dr", [128, 256], F32)
        ps_O = ps2("ps_O", [64, 512], F32)
        ps_o1 = ps2("ps_o1", [128, 1024], F32)

        b_wq = Buf("wq")
        b_wo = Buf("wo")
        b_qc = Buf("qc")
        b_rq = Buf("rq")
        b_qbf = Buf("qbf")
        b_qfm = Buf("qfm")
        b_kfm = Buf("kfm")
        b_sm = Buf("smask")
        b_denr = Buf("denr")
        b_rT = Buf("rT")
        b_div = Buf("divs")
        b_attn = Buf("attn")
        b_o1 = Buf("o1")
        b_tp2 = Buf("tp2")
        b_big = Buf("big")
        b_psS = Buf("psS")
        b_pdr = Buf("pdr")
        b_psO = Buf("psO")
        b_po1 = Buf("po1")

        OP("vector", lambda e: e.memset(qfm[:, :], 0.0), writes=[b_qfm])
        OP("vector", lambda e: e.memset(attn_c[:, :], 0.0), writes=[b_attn])
        OP("vector", lambda e: e.memset(denr[:, :], 0.0), writes=[b_denr])
        OP("vector", lambda e: e.memset(wo_bf[:, :], 0.0), writes=[b_wo])
        dma("gpsimd", wq_bf[:, :].rearrange("p (dc n) -> p dc n", n=1024),
            wqkv.rearrange("(dc p) n -> p dc n", p=128)[:, :, 0:D], writes=[b_wq])
        dma("gpsimd", wo_bf[0:64, :].rearrange("p (h n) -> p h n", n=1024),
            wout.rearrange("(h p) n -> p h n", p=64), writes=[b_wo])

        for c in range(NCH2):
            emit_ln(c, ps_tp2, b_tp2)
            for j in range(2):
                for dc in range(8):
                    OP("tensor", lambda e, j=j, dc=dc: e.matmul(
                        ps_big[:, 512 * j:512 * (j + 1)],
                        xn_fm[:, 128 * dc:128 * (dc + 1)],
                        wq_bf[:, 1024 * dc + 512 * j: 1024 * dc + 512 * (j + 1)],
                        start=(dc == 0), stop=(dc == 7)),
                       reads=[b_xnfm, b_wq], writes=[b_big])
            OP("scalar", lambda e: e.copy(out=qc32[:, :], in_=ps_big[:, 0:D]),
               reads=[b_big], writes=[b_qc])
            emit_ropefeat(c, qc32, b_qc, rq_a, rq_b, b_rq, q_bf[:, :], b_qbf)
            for src_bf, b_src, dst, b_dst in ((q_bf[:, :], b_qbf, qfm, b_qfm),
                                              (k_tm[:, D * c:D * (c + 1)],
                                               b_ktm[c], kfm, b_kfm)):
                for rnd in range(2):
                    for hh in range(8):
                        h = rnd * 8 + hh
                        OP("tensor", lambda e, h=h, hh=hh, s=src_bf: e.transpose(
                            ps_tp2[0:64, 128 * hh:128 * (hh + 1)],
                            s[:, 64 * h:64 * (h + 1)], identb[:, :]),
                           reads=[b_src, b_const], writes=[b_tp2])
                    OP("vector", lambda e, rnd=rnd, d=dst: e.tensor_copy(
                        out=d[0:64, 1024 * rnd:1024 * (rnd + 1)], in_=ps_tp2[0:64, :]),
                       reads=[b_tp2], writes=[b_dst])
            for g in range(4):
                for jj in range(4):
                    h = 4 * g + jj
                    OP("tensor", lambda e, h=h, jj=jj: e.matmul(
                        ps_S[:, 128 * jj:128 * (jj + 1)],
                        kfm[0:64, 128 * h:128 * (h + 1)],
                        qfm[0:64, 128 * h:128 * (h + 1)],
                        start=True, stop=True),
                       reads=[b_kfm, b_qfm], writes=[b_psS])
                OP("vector", lambda e: e.tensor_tensor(
                    out=smask[:, :], in0=ps_S[:, :], in1=masku4[:, :], op=ALU.mult),
                   reads=[b_psS, b_const], writes=[b_sm])
                for jj in range(4):
                    h = 4 * g + jj
                    if LVL < 1:
                        continue
                    OP("tensor", lambda e, h=h, jj=jj: e.matmul(
                        ps_dr[:, jj:jj + 1],
                        smask[:, 128 * jj:128 * (jj + 1)], onesb[:, :],
                        start=True, stop=True),
                       reads=[b_sm, b_const], writes=[b_pdr])
                for jj in range(4):
                    h = 4 * g + jj
                    if LVL < 1:
                        continue
                    OP("tensor", lambda e, h=h, jj=jj: e.matmul(
                        ps_dr[:, 64 + jj:64 + jj + 1],
                        qfm[:, 128 * h:128 * (h + 1)], ksbf[:, h:h + 1],
                        start=True, stop=True),
                       reads=[b_qfm, b_ksbf], writes=[b_pdr])
                for jj in range(4):
                    h = 4 * g + jj
                    if LVL < 2:
                        continue
                    OP("tensor", lambda e, h=h, jj=jj, c=c: e.matmul(
                        ps_O[:, 128 * jj:128 * (jj + 1)],
                        v_tm[:, D * c + 64 * h: D * c + 64 * (h + 1)],
                        smask[:, 128 * jj:128 * (jj + 1)],
                        start=True, stop=False),
                       reads=[b_vtm[c], b_sm], writes=[b_psO])
                    OP("tensor", lambda e, h=h, jj=jj: e.matmul(
                        ps_O[:, 128 * jj:128 * (jj + 1)],
                        KVbf[:, 64 * h:64 * (h + 1)],
                        qfm[:, 128 * h:128 * (h + 1)],
                        start=False, stop=True),
                       reads=[b_kvbf, b_qfm], writes=[b_psO])
                if LVL < 3:
                    continue
                OP("vector", lambda e: e.tensor_copy(
                    out=denA[:, :], in_=ps_dr[:, 0:4]),
                   reads=[b_pdr], writes=[b_denr])
                if debug and c == 0 and g == 0:
                    dma("sync", dbg["d_denA2"], denA[:, :], reads=[b_denr])
                OP("vector", lambda e: e.tensor_tensor(
                    out=denA[:, :], in0=denA[:, :],
                    in1=ps_dr[:, 64:68], op=ALU.add),
                   reads=[b_pdr, b_denr], writes=[b_denr])
                if debug and c == 0 and g == 0:
                    OP("vector", lambda e: e.tensor_copy(out=rq_a[:, 0:68], in_=ps_dr[:, 0:68]),
                       reads=[b_pdr], writes=[b_rq])
                    dma("sync", dbg["d_rawden"], rq_a[:, 0:68], reads=[b_rq])
                    dma("sync", dbg["d_denA"], denA[:, :], reads=[b_denr])
                    OP("vector", lambda e: e.tensor_copy(out=divs[:, :], in_=ps_O[:, :]),
                       reads=[b_psO], writes=[b_div])
                    dma("sync", dbg["d_rawO"], divs[:, :], reads=[b_div])
                OP("vector", lambda e: e.tensor_scalar(
                    out=denr[:, 0:4], in0=denA[:, :],
                    scalar1=1e-6, scalar2=None, op0=ALU.max),
                   reads=[b_denr], writes=[b_denr])
                OP("vector", lambda e: e.reciprocal(denr[:, 0:4], denr[:, 0:4]),
                   reads=[b_denr], writes=[b_denr])
                OP("tensor", lambda e: e.transpose(
                    ps_dr[0:16, 128:256], denr[:, :], identf[:, :]),
                   reads=[b_denr, b_const], writes=[b_pdr])
                OP("vector", lambda e: e.tensor_copy(out=rT[:, :], in_=ps_dr[0:4, 128:256]),
                   reads=[b_pdr], writes=[b_rT])
                for jj in range(4):
                    OP("tensor", lambda e, jj=jj: e.matmul(
                        ps_S[0:64, 128 * jj:128 * (jj + 1)],
                        esel4[:, 64 * jj:64 * (jj + 1)], rT[:, :],
                        start=True, stop=True),
                       reads=[b_rT, b_const], writes=[b_psS])
                OP("vector", lambda e: e.tensor_copy(out=divs[:, :], in_=ps_S[0:64, :]),
                   reads=[b_psS], writes=[b_div])
                OP("vector", lambda e, g=g: e.tensor_tensor(
                    out=attn_c[0:64, 512 * g:512 * (g + 1)], in0=ps_O[:, :],
                    in1=divs[:, :], op=ALU.mult),
                   reads=[b_psO, b_div], writes=[b_attn])
            for h in range(H):
                OP("tensor", lambda e, h=h, c=c: e.matmul(
                    ps_big[:, :][0:64, 64 * h:64 * (h + 1)],
                    k_tm[:, D * c + 64 * h: D * c + 64 * (h + 1)],
                    v_tm[:, D * c + 64 * h: D * c + 64 * (h + 1)],
                    start=(h == 0 or h == 8), stop=(h == 7 or h == 15)),
                   reads=[b_ktm[c], b_vtm[c]], writes=[b_big])
                OP("tensor", lambda e, h=h, c=c: e.matmul(
                    ps_dr[:, :][0:64, 32 + h:33 + h],
                    k_tm[:, D * c + 64 * h: D * c + 64 * (h + 1)], onesb[:, :],
                    start=(h == 0), stop=(h == 15)),
                   reads=[b_ktm[c], b_const], writes=[b_pdr])
            OP("vector", lambda e: e.tensor_tensor(
                out=KV32[:, :], in0=KV32[:, :], in1=ps_big[0:64, :], op=ALU.add),
               reads=[b_big, b_kv32], writes=[b_kv32])
            OP("vector", lambda e: e.tensor_tensor(
                out=ks32[:, :], in0=ks32[:, :], in1=ps_dr[0:64, 32:48], op=ALU.add),
               reads=[b_pdr, b_ks32], writes=[b_ks32])
            OP("vector", lambda e: e.tensor_copy(out=KVbf[0:64, :], in_=KV32[:, :]),
               reads=[b_kv32], writes=[b_kvbf])
            OP("vector", lambda e: e.tensor_copy(out=ksbf[0:64, :], in_=ks32[:, :]),
               reads=[b_ks32], writes=[b_ksbf])
            for ds in (range(8) if LVL >= 9 else []):
                for h in range(H):
                    OP("tensor", lambda e, ds=ds, h=h: e.matmul(
                        ps_o1[:, 128 * ds:128 * (ds + 1)],
                        wo_bf[:, 1024 * h + 128 * ds: 1024 * h + 128 * (ds + 1)],
                        attn_c[:, 128 * h:128 * (h + 1)],
                        start=(h == 0), stop=(h == H - 1)),
                       reads=[b_wo, b_attn], writes=[b_po1])
            if LVL >= 9:
                OP("vector", lambda e: e.tensor_copy(out=o1_bf[:, :], in_=ps_o1[:, :]),
                   reads=[b_po1], writes=[b_o1])
                for ds in range(8):
                    OP("tensor", lambda e, ds=ds: e.transpose(
                        ps_tp2[:, 128 * ds:128 * (ds + 1)],
                        o1_bf[:, 128 * ds:128 * (ds + 1)], identb[:, :]),
                       reads=[b_o1, b_const], writes=[b_tp2])
                OP("vector", lambda e, c=c: e.tensor_tensor(
                    out=x_bf[:, D * c:D * (c + 1)], in0=x_bf[:, D * c:D * (c + 1)],
                    in1=ps_tp2[:, :], op=ALU.add),
                   reads=[b_tp2], writes=[b_x[c]])
            if debug and c == 0:
                dma("gpsimd", dbg["d_attn"], attn_c[0:64, :], reads=[b_attn])
                dma("gpsimd", dbg["d_qfm"], qfm[0:64, :], reads=[b_qfm])
                dma("gpsimd", dbg["d_x1"], x_bf[:, 0:D], reads=[b_x[0]])
                dma("gpsimd", dbg["d_xnfm"], xn_fm[:, :], reads=[b_xnfm])
                dma("sync", dbg["d_den"], divs[:, :], reads=[b_div])
                dma("sync", dbg["d_denr"], denr[:, :], reads=[b_denr])
                dma("sync", dbg["d_rt"], rT[:, :], reads=[b_rT])
                dma("gpsimd", dbg["d_smask"], smask[:, :], reads=[b_sm])
        pb.barrier()

    p12.close()

    pb.region = "ffn"
    with ExitStack() as p3:
        sb3 = lambda name, shape, dt: p3.enter_context(nc.sbuf_tensor(name, shape, dt))
        ps3 = lambda name, shape, dt: p3.enter_context(nc.psum_tensor(name, shape, dt))
        wr_bf = sb3("wr_bf", [128, 8 * 8], BF16)
        wT_sb = sb3("wT_sb", [8, NCH * 128], F32)
        wg_sb = sb3("wg_sb", [128, 8192], BF16)
        wu_sb = sb3("wu_sb", [128, 8192], BF16)
        wd_sb = sb3("wd_sb", [128, 8192], BF16)
        xn2fm = sb3("xn2fm", [128, 8192], BF16)
        h_all = sb3("h_all", [128, 8192], BF16)
        acc = sb3("acc", [128, 8192], F32)
        wb_sb = sb3("wb_sb", [128, 1024], F32)
        sg_sb = sb3("sg_sb", [128, 512], F32)
        hp_sb = sb3("hp_sb", [128, 512], F32)
        out_st = sb3("out_st", [128, 1024], F32)
        rt = {nm: sb3("rt_" + nm, [128, 8], F32) for nm in
              ("e", "inh", "z1", "z2", "zone", "dist", "w", "fb", "m1x", "fb2",
               "fbw", "wm", "wfin")}
        rs = {nm: sb3("rs_" + nm, [128, 1], F32) for nm in
              ("wsum", "nex", "nex1", "m1", "m2", "fbs", "rec", "ws2", "rec2", "ntr")}

        ps_tp3 = ps3("ps_tp3", [128, 1024], BF16)
        ps_g = ps3("ps_g", [128, 512], F32)
        ps_u = ps3("ps_u", [128, 512], F32)
        ps_dn = ps3("ps_dn", [128, 1024], F32)
        ps_ft = ps3("ps_ft", [128, 1024], F32)

        b_wr = Buf("wr")
        b_wT = Buf("wT")
        b_wg = Buf("wg")
        b_wu = Buf("wu")
        b_wd = Buf("wd")
        b_xn2 = Buf("xn2")
        b_h = Buf("h_all")
        b_acc = Buf("acc")
        b_wb = Buf("wb")
        b_sg = Buf("sg")
        b_hp = Buf("hp")
        b_ost = Buf("ost")
        b_rt = Buf("rt")
        b_tp3 = Buf("tp3")
        b_pg = Buf("pg")
        b_pu = Buf("pu")
        b_pdn = Buf("pdn")
        b_pft = Buf("pft")

        if "wdma" not in skips:
            dma("gpsimd", wr_bf[:, :].rearrange("p (dc n) -> p dc n", n=8),
                wr_h.rearrange("(dc p) n -> p dc n", p=128), writes=[b_wr])

        OP("vector", lambda e: e.reciprocal(rs["ntr"][:, :], temp_sb[:, :]),
           reads=[b_const], writes=[b_rt])
        OP("vector", lambda e: e.tensor_scalar(
            out=rs["ntr"][:, :], in0=rs["ntr"][:, :], scalar1=-1.0, scalar2=None,
            op0=ALU.mult), reads=[b_rt], writes=[b_rt])

        def emit_routing(c):
            hf = c // 8
            tc = c % 8
            emit_ln(c, ps_tp3, b_tp3)
            OP("vector", lambda e, tc=tc: e.tensor_copy(
                out=xn2fm[:, :].rearrange("p (dc n) -> p dc n", n=1024)[:, :, 128 * tc:128 * (tc + 1)],
                in_=ps_tp3[:, :].rearrange("p (dc n) -> p dc n", n=128)),
               reads=[b_tp3], writes=[b_xn2])
            for dc in range(8):
                OP("tensor", lambda e, dc=dc: e.matmul(
                    ps_dn[:, 0:8], xn_fm[:, 128 * dc:128 * (dc + 1)],
                    wr_bf[:, 8 * dc:8 * (dc + 1)],
                    start=(dc == 0), stop=(dc == 7)),
                   reads=[b_xnfm, b_wr], writes=[b_pdn])
            V = "vector"
            S = "scalar"
            OP(S, lambda e: e.activation(out=rt["e"][:, :], in_=ps_dn[:, 0:8],
                                         func=AF.Exp, scale=rs["ntr"][:, :]),
               reads=[b_pdn, b_rt], writes=[b_rt])
            OP(V, lambda e: e.tensor_scalar(out=rt["inh"][:, :], in0=rt["e"][:, :],
                                            scalar1=1.0, scalar2=None, op0=ALU.add),
               reads=[b_rt], writes=[b_rt])
            OP(V, lambda e: e.reciprocal(rt["inh"][:, :], rt["inh"][:, :]),
               reads=[b_rt], writes=[b_rt])
            OP(V, lambda e: e.tensor_single_scalar(out=rt["z1"][:, :], in_=rt["inh"][:, :],
                                                   scalar=GL, op=ALU.is_ge),
               reads=[b_rt], writes=[b_rt])
            OP(V, lambda e: e.tensor_single_scalar(out=rt["z2"][:, :], in_=rt["inh"][:, :],
                                                   scalar=GU, op=ALU.is_le),
               reads=[b_rt], writes=[b_rt])
            OP(V, lambda e: e.tensor_tensor(out=rt["zone"][:, :], in0=rt["z1"][:, :],
                                            in1=rt["z2"][:, :], op=ALU.mult),
               reads=[b_rt], writes=[b_rt])
            OP(V, lambda e: e.tensor_scalar(out=rt["dist"][:, :], in0=rt["inh"][:, :],
                                            scalar1=GC, scalar2=None, op0=ALU.subtract),
               reads=[b_rt], writes=[b_rt])
            OP(S, lambda e: e.activation(out=rt["dist"][:, :], in_=rt["dist"][:, :],
                                         func=AF.Abs),
               reads=[b_rt], writes=[b_rt])
            OP(S, lambda e: e.activation(out=rt["w"][:, :], in_=rt["dist"][:, :],
                                         func=AF.Exp, scale=C_N10()),
               reads=[b_rt, b_const], writes=[b_rt])
            OP(V, lambda e: e.tensor_tensor(out=rt["w"][:, :], in0=rt["w"][:, :],
                                            in1=rt["zone"][:, :], op=ALU.mult),
               reads=[b_rt], writes=[b_rt])
            OP(V, lambda e: e.reduce_sum(rs["wsum"][:, :], rt["w"][:, :], axis=AX),
               reads=[b_rt], writes=[b_rt])
            OP(V, lambda e: e.tensor_single_scalar(out=rs["nex"][:, :], in_=rs["wsum"][:, :],
                                                   scalar=1e-8, op=ALU.is_lt),
               reads=[b_rt], writes=[b_rt])
            OP(S, lambda e: e.activation(out=rt["fb"][:, :], in_=rt["dist"][:, :],
                                         func=AF.Exp, scale=C_N103()),
               reads=[b_rt, b_const], writes=[b_rt])
            OP(V, lambda e: e.reduce_max(rs["m1"][:, :], rt["fb"][:, :], axis=AX),
               reads=[b_rt], writes=[b_rt])
            OP(V, lambda e: e.tensor_scalar(out=rt["m1x"][:, :], in0=rt["fb"][:, :],
                                            scalar1=rs["m1"][:, :], scalar2=None, op0=ALU.is_ge),
               reads=[b_rt], writes=[b_rt], fence=True)
            OP(V, lambda e: e.tensor_scalar(out=rt["fb2"][:, :], in0=rt["fb"][:, :],
                                            scalar1=1.0, scalar2=None, op0=ALU.add),
               reads=[b_rt], writes=[b_rt])
            OP(V, lambda e: e.tensor_tensor(out=rt["fb2"][:, :], in0=rt["fb2"][:, :],
                                            in1=rt["m1x"][:, :], op=ALU.mult),
               reads=[b_rt], writes=[b_rt])
            OP(V, lambda e: e.tensor_tensor(out=rt["fb2"][:, :], in0=rt["fb"][:, :],
                                            in1=rt["fb2"][:, :], op=ALU.subtract),
               reads=[b_rt], writes=[b_rt])
            OP(V, lambda e: e.reduce_max(rs["m2"][:, :], rt["fb2"][:, :], axis=AX),
               reads=[b_rt], writes=[b_rt])
            OP(V, lambda e: e.tensor_scalar(out=rt["m1x"][:, :], in0=rt["fb"][:, :],
                                            scalar1=rs["m2"][:, :], scalar2=None, op0=ALU.is_ge),
               reads=[b_rt], writes=[b_rt], fence=True)
            OP(V, lambda e: e.tensor_tensor(out=rt["fbw"][:, :], in0=rt["fb"][:, :],
                                            in1=rt["m1x"][:, :], op=ALU.mult),
               reads=[b_rt], writes=[b_rt])
            OP(V, lambda e: e.reduce_sum(rs["fbs"][:, :], rt["fbw"][:, :], axis=AX),
               reads=[b_rt], writes=[b_rt])
            OP(V, lambda e: e.tensor_single_scalar(out=rs["fbs"][:, :], in_=rs["fbs"][:, :],
                                                   scalar=1e-8, op=ALU.max),
               reads=[b_rt], writes=[b_rt])
            OP(V, lambda e: e.reciprocal(rs["rec"][:, :], rs["fbs"][:, :]),
               reads=[b_rt], writes=[b_rt])
            OP(V, lambda e: e.tensor_scalar(out=rt["fbw"][:, :], in0=rt["fbw"][:, :],
                                            scalar1=rs["rec"][:, :], scalar2=None, op0=ALU.mult),
               reads=[b_rt], writes=[b_rt], fence=True)
            OP(V, lambda e: e.tensor_scalar(out=rs["nex1"][:, :], in0=rs["nex"][:, :],
                                            scalar1=-1.0, scalar2=1.0, op0=ALU.mult, op1=ALU.add),
               reads=[b_rt], writes=[b_rt])
            OP(V, lambda e: e.tensor_scalar(out=rt["w"][:, :], in0=rt["w"][:, :],
                                            scalar1=rs["nex1"][:, :], scalar2=None, op0=ALU.mult),
               reads=[b_rt], writes=[b_rt], fence=True)
            OP(V, lambda e: e.tensor_scalar(out=rt["fbw"][:, :], in0=rt["fbw"][:, :],
                                            scalar1=rs["nex"][:, :], scalar2=None, op0=ALU.mult),
               reads=[b_rt], writes=[b_rt], fence=True)
            OP(V, lambda e: e.tensor_tensor(out=rt["wm"][:, :], in0=rt["w"][:, :],
                                            in1=rt["fbw"][:, :], op=ALU.add),
               reads=[b_rt], writes=[b_rt])
            OP(V, lambda e: e.reduce_sum(rs["ws2"][:, :], rt["wm"][:, :], axis=AX),
               reads=[b_rt], writes=[b_rt])
            OP(V, lambda e: e.tensor_single_scalar(out=rs["ws2"][:, :], in_=rs["ws2"][:, :],
                                                   scalar=1e-8, op=ALU.max),
               reads=[b_rt], writes=[b_rt])
            OP(V, lambda e: e.reciprocal(rs["rec2"][:, :], rs["ws2"][:, :]),
               reads=[b_rt], writes=[b_rt])
            OP(V, lambda e: e.tensor_scalar(out=rt["wfin"][:, :], in0=rt["wm"][:, :],
                                            scalar1=rs["rec2"][:, :], scalar2=None, op0=ALU.mult),
               reads=[b_rt], writes=[b_rt], fence=True)
            OP("tensor", lambda e: e.transpose(ps_dn[0:8, 512:640], rt["wfin"][:, :],
                                               identf[:, :]),
               reads=[b_rt, b_const], writes=[b_pdn])
            OP("vector", lambda e, c=c: e.tensor_copy(
                out=wT_sb[:, 128 * c:128 * (c + 1)], in_=ps_dn[0:8, 512:640]),
               reads=[b_pdn], writes=[b_wT])
            if debug and c == 0:
                dma("sync", dbg["d_wfin"], rt["wfin"][:, :], reads=[b_rt])

        for hf in range(0 if "ffn" in skips else 2):
            for tc in range(8):
                emit_routing(hf * 8 + tc)
            for ei in range(E):
                dma("gpsimd", wg_sb[:, :].rearrange("p (dc n) -> p dc n", n=1024),
                    wg_h[ei].rearrange("(dc p) n -> p dc n", p=128), writes=[b_wg])
                dma("gpsimd", wu_sb[:, :].rearrange("p (dc n) -> p dc n", n=1024),
                    wu_h[ei].rearrange("(dc p) n -> p dc n", p=128), writes=[b_wu])
                dma("gpsimd", wd_sb[:, :].rearrange("p (ic n) -> p ic n", n=1024),
                    wd_h[ei].rearrange("(ic p) n -> p ic n", p=128), writes=[b_wd])
                for j in range(2):
                    OP("tensor", lambda e, j=j, ei=ei, hf=hf: e.matmul(
                        ps_ft[:, 512 * j:512 * (j + 1)],
                        e8sel[:, 128 * ei:128 * (ei + 1)],
                        wT_sb[:, 1024 * hf + 512 * j: 1024 * hf + 512 * (j + 1)],
                        start=True, stop=True),
                       reads=[b_wT, b_const], writes=[b_pft])
                OP("vector", lambda e: e.tensor_copy(out=wb_sb[:, :], in_=ps_ft[:, :]),
                   reads=[b_pft], writes=[b_wb])
                for ic in range(8):
                    for tt in range(2):
                        for dc in range(8):
                            OP("tensor", lambda e, ic=ic, tt=tt, dc=dc: e.matmul(
                                ps_g[:, :],
                                wg_sb[:, 1024 * dc + 128 * ic: 1024 * dc + 128 * (ic + 1)],
                                xn2fm[:, 1024 * dc + 512 * tt: 1024 * dc + 512 * (tt + 1)],
                                start=(dc == 0), stop=(dc == 7)),
                               reads=[b_wg, b_xn2], writes=[b_pg])
                        for dc in range(8):
                            OP("tensor", lambda e, ic=ic, tt=tt, dc=dc: e.matmul(
                                ps_u[:, :],
                                wu_sb[:, 1024 * dc + 128 * ic: 1024 * dc + 128 * (ic + 1)],
                                xn2fm[:, 1024 * dc + 512 * tt: 1024 * dc + 512 * (tt + 1)],
                                start=(dc == 0), stop=(dc == 7)),
                               reads=[b_wu, b_xn2], writes=[b_pu])
                        OP("scalar", lambda e: e.activation(out=sg_sb[:, :], in_=ps_g[:, :],
                                                            func=AF.Silu),
                           reads=[b_pg], writes=[b_sg])
                        OP("vector", lambda e: e.tensor_tensor(out=hp_sb[:, :], in0=sg_sb[:, :],
                                                               in1=ps_u[:, :], op=ALU.mult),
                           reads=[b_sg, b_pu], writes=[b_hp])
                        OP("vector", lambda e, ic=ic, tt=tt: e.tensor_tensor(
                            out=h_all[:, 1024 * ic + 512 * tt: 1024 * ic + 512 * (tt + 1)],
                            in0=hp_sb[:, :], in1=wb_sb[:, 512 * tt:512 * (tt + 1)],
                            op=ALU.mult),
                           reads=[b_hp, b_wb], writes=[b_h])
                for tt in range(2):
                    for dc in range(8):
                        half_dn = (dc % 2)
                        for ic in range(8):
                            OP("tensor", lambda e, tt=tt, dc=dc, ic=ic, hd=half_dn: e.matmul(
                                ps_dn[:, 512 * hd:512 * (hd + 1)],
                                wd_sb[:, 1024 * ic + 128 * dc: 1024 * ic + 128 * (dc + 1)],
                                h_all[:, 1024 * ic + 512 * tt: 1024 * ic + 512 * (tt + 1)],
                                start=(ic == 0), stop=(ic == 7)),
                               reads=[b_wd, b_h], writes=[b_pdn])
                        if ei == 0:
                            OP("vector", lambda e, tt=tt, dc=dc, hd=half_dn: e.tensor_copy(
                                out=acc[:, 1024 * dc + 512 * tt: 1024 * dc + 512 * (tt + 1)],
                                in_=ps_dn[:, 512 * hd:512 * (hd + 1)]),
                               reads=[b_pdn], writes=[b_acc])
                        else:
                            OP("vector", lambda e, tt=tt, dc=dc, hd=half_dn: e.tensor_tensor(
                                out=acc[:, 1024 * dc + 512 * tt: 1024 * dc + 512 * (tt + 1)],
                                in0=acc[:, 1024 * dc + 512 * tt: 1024 * dc + 512 * (tt + 1)],
                                in1=ps_dn[:, 512 * hd:512 * (hd + 1)], op=ALU.add),
                               reads=[b_pdn, b_acc], writes=[b_acc])
            for tc in range(8):
                c = hf * 8 + tc
                for dc in range(8):
                    OP("tensor", lambda e, tc=tc, dc=dc: e.transpose(
                        ps_ft[:, 128 * dc:128 * (dc + 1)],
                        acc[:, 1024 * dc + 128 * tc: 1024 * dc + 128 * (tc + 1)],
                        identf[:, :]),
                       reads=[b_acc, b_const], writes=[b_pft])
                OP("vector", lambda e, c=c: e.tensor_tensor(
                    out=out_st[:, :], in0=x_bf[:, D * c:D * (c + 1)],
                    in1=ps_ft[:, :], op=ALU.add),
                   reads=[b_pft, b_x[c]], writes=[b_ost])
                out_toks.append(dma("sync", out_h[128 * c:128 * (c + 1), :],
                                    out_st[:, :], reads=[b_ost]))

    if "ffn" in skips:
        for c in range(NCH):
            out_toks.append(dma("gpsimd", out_h[128 * c:128 * (c + 1), :],
                                x_bf[:, D * c:D * (c + 1)], reads=[b_x[c]]))
    OP("gpsimd", lambda e: e.engine_nop(), deps=list(out_toks))

    pb.finalize_and_emit(nc, ctx)
    return nc


_NC_CACHE = {}


def _get_nc(debug=False):
    key = bool(debug)
    if key not in _NC_CACHE:
        _NC_CACHE[key] = build_program(debug=key)
    return _NC_CACHE[key]


def make_inmaps(x_full, wqkv, wout, wg, wu, wd, wr, temp):
    x_full = np.ascontiguousarray(x_full.reshape(NCORES * M, D).astype(np.float32))
    half = 32
    inv = 1.0 / (10000.0 ** (np.arange(half, dtype=np.float64) / half))
    masku = (np.arange(128)[:, None] <= np.arange(128)[None, :]).astype(np.float32)
    masku4 = np.tile(masku, (1, 4)).copy()
    esel4 = np.zeros((4, 256), np.float32)
    for j in range(4):
        esel4[j, 64 * j:64 * (j + 1)] = 1.0
    e8sel = np.zeros((8, 1024), np.float32)
    for j in range(8):
        e8sel[j, 128 * j:128 * (j + 1)] = 1.0
    identb = np.eye(128).astype(ml_dtypes.bfloat16)
    identf = np.eye(128, dtype=np.float32)
    onesb = np.ones((128, 1), ml_dtypes.bfloat16)
    temp1 = np.full((128, 1), float(temp), np.float32)
    consts = np.zeros((128, 8), np.float32)
    consts[:, 0] = 1.0 / D
    consts[:, 1] = 1e-5
    consts[:, 2] = SCALE
    consts[:, 3] = -10.0
    consts[:, 4] = -10.0 / 3.0
    in_maps = []
    for i in range(NCORES):
        pos = np.arange(i * M, (i + 1) * M, dtype=np.float64)[:, None] * inv[None, :]
        cosr = np.cos(pos).astype(np.float32)
        sinr = np.sin(pos).astype(np.float32)
        prefmask = np.zeros((64, 8), np.float32)
        prefmask[:, :i] = 1.0
        in_maps.append({
            "x": x_full[i * M:(i + 1) * M],
            "wqkv": wqkv, "wout": wout, "wg": wg, "wu": wu, "wd": wd, "wr": wr,
            "temp1": temp1, "cosr": cosr, "sinr": sinr,
            "identb": identb, "identf": identf, "masku4": masku4,
            "onesb": onesb, "esel4": esel4, "e8sel": e8sel,
            "prefmask": prefmask, "consts": consts,
        })
    return in_maps


def kernel(L0_attn_x, L0_attn_w_qkv, L0_attn_w_out, L0_ffn_Wg, L0_ffn_Wu,
           L0_ffn_Wd, L0_ffn_Wr, L0_ffn_temperature):
    x = np.asarray(L0_attn_x)
    in_maps = make_inmaps(x, np.asarray(L0_attn_w_qkv, np.float32),
                          np.asarray(L0_attn_w_out, np.float32),
                          np.asarray(L0_ffn_Wg, np.float32),
                          np.asarray(L0_ffn_Wu, np.float32),
                          np.asarray(L0_ffn_Wd, np.float32),
                          np.asarray(L0_ffn_Wr, np.float32),
                          float(np.asarray(L0_ffn_temperature).reshape(-1)[0]))
    nc = _get_nc(debug=False)
    res = run_bass_kernel_spmd(nc, in_maps, core_ids=list(range(NCORES)))
    out = np.concatenate([r["out"] for r in res.results], axis=0)
    return out.reshape(1, NCORES * M, D).astype(np.float32)
